# version 17
# baseline (speedup 1.0000x reference)
import contextlib
import numpy as np
import ml_dtypes
import concourse.bass as bass
import concourse.mybir as mybir
from concourse.bass_utils import run_bass_kernel_spmd

F32 = mybir.dt.float32
BF16 = mybir.dt.bfloat16
AF = mybir.ActivationFunctionType
ALU = mybir.AluOpType
AX = mybir.AxisListType

R = 2048
D = 4096
NTC = R // 128
NSB = 2
TCSB = NTC // NSB
NOB = 8
NDC = 32
EPS = 1e-5
LN2C = 0.6931471805599453


def build_nc():
    nc = bass.Bass("TRN2", target_bir_lowering=False,
                   detect_race_conditions=False)

    xs = nc.dram_tensor("xs", [R, D], F32, kind="ExternalInput")
    wbT = nc.dram_tensor("wbT", [D, D], F32, kind="ExternalInput")
    bb = nc.dram_tensor("bb", [D], F32, kind="ExternalInput")
    wrT = nc.dram_tensor("wrT", [D, 8], F32, kind="ExternalInput")
    acatT = nc.dram_tensor("acatT", [D, 128], F32, kind="ExternalInput")
    bcat = nc.dram_tensor("bcat", [128, D], F32, kind="ExternalInput")
    identb = nc.dram_tensor("identb", [128, 128], BF16, kind="ExternalInput")
    identf = nc.dram_tensor("identf", [128, 128], F32, kind="ExternalInput")
    eopat = nc.dram_tensor("eopat", [8, 128], BF16, kind="ExternalInput")
    ones1 = nc.dram_tensor("ones1", [1, 128], BF16, kind="ExternalInput")
    out = nc.dram_tensor("out", [R, D], F32, kind="ExternalOutput")
    ffnout = nc.dram_tensor("ffnout", [R, D], BF16)

    stp = contextlib.ExitStack()
    sbp = lambda n, s, d: stp.enter_context(nc.sbuf_tensor(n, s, d))

    y1fm = sbp("y1fm", [128, NDC, R], BF16)
    ident_b = sbp("ident_b", [128, 128], BF16)
    ident_f = sbp("ident_f", [128, 128], F32)
    eop = sbp("eop", [8, 128], BF16)
    one1 = sbp("one1", [1, 128], BF16)
    wr_sb = sbp("wr_sb", [128, NDC, 8], BF16)
    acat_sb = sbp("acat_sb", [128, NDC, 128], BF16)
    bcat_sb = sbp("bcat_sb", [128, D], BF16)
    bb_sb = sbp("bb_sb", [1, D], BF16)

    S = {}
    for name in ["c_ld", "c_ld2", "x", "xfree", "v2s", "s2v", "y1tm", "fm",
                 "wt", "wfree", "ps", "evac", "stg", "ffst",
                 "lgmm", "hmm", "lgev", "ltT", "exp1", "exp2",
                 "wrtv", "wrtT", "wTev", "wbc", "hp",
                 "ln2b", "nrm", "nmrs", "rl2", "nrmd", "outd"]:
        S[name] = nc.alloc_semaphore("S_" + name)

    sem_nums = sorted(h.num for h in S.values())
    assert sem_nums[-1] - sem_nums[0] + 1 == len(sem_nums)
    with nc.Block() as blk:
        @blk.gpsimd
        def _(g):
            g.sem_clear(range(sem_nums[0], sem_nums[-1] + 1))

    with nc.Block() as blk:
        @blk.sync
        def _(sy):
            sy.dma_start(ident_b[:], identb.ap()).then_inc(S["c_ld"], 16)
            sy.dma_start(ident_f[:], identf.ap()).then_inc(S["c_ld"], 16)
            sy.dma_start(eop[:], eopat.ap()).then_inc(S["c_ld"], 16)
            sy.dma_start(one1[:], ones1.ap()).then_inc(S["c_ld"], 16)

        @blk.gpsimd
        def _(g):
            g.dma_start(wr_sb[:], wrT.ap().rearrange("(dc p) e -> p dc e", p=128)
                        ).then_inc(S["c_ld2"], 16)
            g.dma_start(acat_sb[:], acatT.ap().rearrange("(dc p) e -> p dc e", p=128)
                        ).then_inc(S["c_ld2"], 16)
            g.dma_start(bcat_sb[:], bcat.ap()).then_inc(S["c_ld2"], 16)
            g.dma_start(bb_sb[0:1, :], bb.ap().rearrange("(a d) -> a d", a=1)).then_inc(S["c_ld2"], 16)
            g.wait_ge(S["c_ld2"], 16 * 4)
            g.wait_ge(S["c_ld"], 16 * 4)

    with contextlib.ExitStack() as p1:
        p1s = lambda n, s, d: p1.enter_context(nc.sbuf_tensor(n, s, d))
        xbuf = p1s("xbuf", [128, 2, D], F32)
        y1tm = p1s("y1tm", [128, 2, D], BF16)
        bn6 = p1s("bn6", [128, 48], F32)
        bn2 = p1s("bn2", [128, 2], F32)
        vrec = p1s("vrec", [128, 1], F32)
        vtmp = p1s("vtmp", [128, 1], F32)
        rsb = p1s("rsb", [128, 2], F32)
        ln2t = p1s("ln2t", [128, 1], F32)

        with nc.Block() as blk:
            @blk.sync
            def _(sy):
                def xbar(t):
                    sy.wait_ge(S["y1tm"], t + 1)
                    for s4 in range(4):
                        sy.dma_start(
                            y1fm[:, :, t * 128 + 32 * s4: t * 128 + 32 * s4 + 32],
                            y1tm[32 * s4:32 * (s4 + 1), t % 2],
                            transpose=True,
                        ).then_inc(S["fm"], 16)
                    sy.wait_ge(S["fm"], 16 * 4 * (t + 1))
                for tc in range(NTC):
                    if tc >= 2:
                        sy.wait_ge(S["y1tm"], tc - 1)
                    sy.dma_start(xbuf[:, tc % 2],
                                 xs.ap()[tc * 128:(tc + 1) * 128, :]
                                 ).then_inc(S["x"], 16)
                    sy.wait_ge(S["x"], 16 * (tc + 1))
                    if tc >= 1:
                        xbar(tc - 1)
                xbar(NTC - 1)

            @blk.vector
            def _(v):
                v.memset(ln2t[:], LN2C)
                for tc in range(NTC):
                    v.wait_ge(S["x"], 16 * (tc + 1))
                    xc = xbuf[:, tc % 2]
                    for k in range(8):
                        v.bn_stats(bn6[:, 6 * k:6 * k + 6],
                                   xc[:, 512 * k:512 * (k + 1)])
                    v.drain()
                    v.bn_aggr(bn2[:], bn6[:])
                    v.drain()
                    v.tensor_scalar(vtmp[:], bn2[:, 1:2], 4.0, EPS,
                                    ALU.mult, ALU.add)
                    v.drain().then_inc(S["v2s"], 1)
                    v.wait_ge(S["s2v"], tc + 1)
                    if tc >= 2:
                        v.wait_ge(S["fm"], 16 * 4 * (tc - 1))
                    v.tensor_scalar(y1tm[:, tc % 2], xc, bn2[:, 0:1],
                                    rsb[:, tc % 2:tc % 2 + 1],
                                    ALU.subtract, ALU.mult)
                    v.drain().then_inc(S["y1tm"], 1)

            @blk.scalar
            def _(s):
                for tc in range(NTC):
                    s.wait_ge(S["v2s"], tc + 1)
                    s.activation(vrec[:], vtmp[:], AF.Ln)
                    s.drain()
                    s.activation(rsb[:, tc % 2:tc % 2 + 1], vrec[:],
                                 AF.Exp, scale=-0.5, bias=ln2t[:, 0:1])
                    s.drain().then_inc(S["s2v"], 1)

    with contextlib.ExitStack() as p2:
        p2s = lambda n, s, d: p2.enter_context(nc.sbuf_tensor(n, s, d))
        NW = 4
        wt = p2s("wt", [128, 4, NW * 512], BF16)
        stg = p2s("stg", [128, 4, 512], BF16)
        lg_sb = p2s("lg_sb", [8, 1024], F32)
        ltm = p2s("ltm", [128, TCSB, 8], F32)
        rmx = p2s("rmx", [128, TCSB, 1], F32)
        rsum = p2s("rsum", [128, TCSB, 1], F32)
        rtmp = p2s("rtmp", [128, TCSB, 1], F32)
        prb = p2s("prb", [128, TCSB, 8], F32)
        pm1 = p2s("pm1", [128, TCSB, 1], F32)
        pmsk = p2s("pmsk", [128, TCSB, 8], F32)
        ptmp = p2s("ptmp", [128, TCSB, 8], F32)
        pm2 = p2s("pm2", [128, TCSB, 1], F32)
        wrtw = p2s("wrtw", [128, TCSB, 8], F32)
        wT_sb = p2s("wT_sb", [8, 1024], BF16)
        h_sb = p2s("h_sb", [128, 1024], F32)
        hp_sb = p2s("hp_sb", [128, 1024], BF16)
        stats = p2s("stats", [128, TCSB, NOB, 6], F32)
        m2 = p2s("m2", [128, TCSB, 2], F32)
        vrec2 = p2s("vrec2", [128, TCSB], F32)
        lt2 = p2s("lt2", [128, 1], F32)
        rs2 = p2s("rs2", [128, NTC], F32)
        nmrs = p2s("nmrs", [128, NTC], F32)
        rlb = p2s("rlb", [128, 4, 512], BF16)
        outb = p2s("outb", [128, 4, 512], F32)
        psum_a = p2.enter_context(nc.psum_tensor("psum_a", [128, 8, 512], F32))

        NQ = 64 // NW
        WDC = NDC // NW

        with nc.Block() as blk:
            @blk.gpsimd
            def _(g):
                k = 0
                for sb in range(NSB):
                    for ob in range(NOB):
                        for wq in range(WDC):
                            if k >= 4:
                                g.wait_ge(S["wfree"], k - 3)
                            g.dma_start(
                                wt[:, k % 4].rearrange("p (a b) -> p a b", a=NW),
                                wbT.ap().rearrange("(dc p) o -> p dc o", p=128)
                                [:, wq * NW:(wq + 1) * NW,
                                 ob * 512:(ob + 1) * 512],
                            ).then_inc(S["wt"], 16)
                            k += 1

            @blk.tensor
            def _(t):
                t.wait_ge(S["fm"], 16 * 4 * NTC)
                k = 0
                nevac = 0
                for sb in range(NSB):
                    t0 = sb * 1024
                    if sb > 0:
                        t.wait_ge(S["stg"], 64 * sb)
                    for half in range(2):
                        for dc in range(NDC):
                            mm = t.matmul(psum_a[0:8, half, :], wr_sb[:, dc],
                                     y1fm[:, dc, t0 + 512 * half:t0 + 512 * (half + 1)],
                                     start=(dc == 0), stop=(dc == NDC - 1))
                    t.drain().then_inc(S["lgmm"], 1)
                    for half in range(2):
                        for dc in range(NDC):
                            mm = t.matmul(psum_a[:, 2 + half, :], acat_sb[:, dc],
                                     y1fm[:, dc, t0 + 512 * half:t0 + 512 * (half + 1)],
                                     start=(dc == 0), stop=(dc == NDC - 1))
                    t.drain().then_inc(S["hmm"], 1)
                    t.wait_ge(S["lgev"], sb + 1)
                    for q in range(TCSB):
                        mm = t.transpose(psum_a[:, 4, q * 8:(q + 1) * 8],
                                    lg_sb[:, q * 128:(q + 1) * 128],
                                    ident_f[0:8, 0:8])
                    t.drain().then_inc(S["ltT"], 1)
                    t.wait_ge(S["wrtv"], sb + 1)
                    for q in range(TCSB):
                        mm = t.transpose(psum_a[0:8, 4 + q // 4, (q % 4) * 128:
                                           (q % 4) * 128 + 128],
                                    wrtw[:, q], ident_f[:])
                    t.drain().then_inc(S["wrtT"], 1)
                    t.wait_ge(S["wTev"], sb + 1)
                    for half in range(2):
                        mm = t.matmul(psum_a[:, 6 + half, :], eop[:],
                                 wT_sb[:, 512 * half:512 * (half + 1)],
                                 start=True, stop=True)
                    t.drain().then_inc(S["wbc"], 1)
                    t.wait_ge(S["hp"], sb + 1)
                    for ob in range(NOB):
                        for dc in range(NDC):
                            if dc % NW == 0:
                                t.wait_ge(S["wt"], 16 * (k + 1))
                            wslice = wt[:, k % 4, (dc % NW) * 512:
                                        (dc % NW) * 512 + 512]
                            for q in range(TCSB):
                                if ob >= 1 and dc == 0:
                                    t.wait_ge(S["stg"], nevac - TCSB + q + 1)
                                last_mm = t.matmul(
                                    psum_a[:, q, :],
                                    y1fm[:, dc, t0 + q * 128:t0 + (q + 1) * 128],
                                    wslice, start=(dc == 0), stop=False)
                                if ob * 512 <= dc * 128 < (ob + 1) * 512:
                                    off = dc * 128 - ob * 512
                                    t.matmul(psum_a[:, q, off:off + 128],
                                             y1fm[:, dc, t0 + q * 128:t0 + (q + 1) * 128],
                                             ident_b[:], start=False, stop=False)
                            if dc % NW == NW - 1:
                                last_mm.then_inc(S["wfree"], 1)
                                k += 1
                        for q in range(TCSB):
                            t.matmul(psum_a[:, q, :],
                                     hp_sb[:, q * 128:(q + 1) * 128],
                                     bcat_sb[:, ob * 512:(ob + 1) * 512],
                                     start=False, stop=False)
                            t.matmul(psum_a[:, q, :], one1[:],
                                     bb_sb[:, ob * 512:(ob + 1) * 512],
                                     start=False, stop=True)
                        t.drain().then_inc(S["ps"], 1)
                        nevac += TCSB

            @blk.vector
            def _(v):
                nps = 0
                nevac = 0
                for sb in range(NSB):
                    v.wait_ge(S["lgmm"], sb + 1)
                    v.tensor_copy(lg_sb[:],
                                  psum_a[0:8, 0:2, :].rearrange("p a b -> p (a b)"))
                    v.drain().then_inc(S["lgev"], 1)
                    v.wait_ge(S["ltT"], sb + 1)
                    v.tensor_copy(ltm[:], psum_a[:, 4, 0:TCSB * 8].rearrange(
                        "p (a b) -> p a b", b=8))
                    v.drain()
                    v.reduce_max(rmx[:], ltm[:], axis=AX.X)
                    v.drain()
                    v.tensor_tensor(prb[:], ltm[:],
                                    rmx[:].broadcast_to((128, TCSB, 8)), ALU.subtract)
                    v.drain().then_inc(S["exp1"], 1)
                    v.wait_ge(S["exp2"], sb + 1)
                    v.reduce_max(pm1[:], prb[:], axis=AX.X)
                    v.drain()
                    v.tensor_tensor(pmsk[:], prb[:],
                                    pm1[:].broadcast_to((128, TCSB, 8)), ALU.is_equal)
                    v.drain()
                    v.tensor_tensor(ptmp[:], prb[:], pmsk[:], ALU.mult)
                    v.drain()
                    v.tensor_tensor(ptmp[:], prb[:], ptmp[:], ALU.subtract)
                    v.drain()
                    v.reduce_max(pm2[:], ptmp[:], axis=AX.X)
                    v.drain()
                    v.tensor_tensor(pmsk[:], prb[:],
                                    pm2[:].broadcast_to((128, TCSB, 8)), ALU.is_ge)
                    v.drain()
                    v.tensor_tensor(wrtw[:], prb[:], pmsk[:], ALU.mult)
                    v.tensor_add(rtmp[:], pm1[:], pm2[:])
                    v.drain()
                    v.reciprocal(pm1[:], rtmp[:])
                    v.drain()
                    v.tensor_scalar_mul(pm1[:], pm1[:], 2.0)
                    v.drain()
                    v.tensor_tensor(wrtw[:], wrtw[:],
                                    pm1[:].broadcast_to((128, TCSB, 8)), ALU.mult)
                    v.drain().then_inc(S["wrtv"], 1)
                    v.wait_ge(S["wrtT"], sb + 1)
                    v.tensor_copy(wT_sb[:], psum_a[0:8, 4:6, :].rearrange(
                        "p a b -> p (a b)"))
                    v.drain().then_inc(S["wTev"], 1)
                    v.wait_ge(S["hmm"], sb + 1)
                    v.wait_ge(S["wbc"], sb + 1)
                    v.tensor_copy(h_sb[:],
                                  psum_a[:, 2:4, :].rearrange("p a b -> p (a b)"))
                    v.drain()
                    v.tensor_tensor(hp_sb[:, 0:512], h_sb[:, 0:512],
                                    psum_a[:, 6, :], ALU.mult)
                    v.tensor_tensor(hp_sb[:, 512:1024], h_sb[:, 512:1024],
                                    psum_a[:, 7, :], ALU.mult)
                    v.drain().then_inc(S["hp"], 1)
                    for ob in range(NOB):
                        nps += 1
                        v.wait_ge(S["ps"], nps)
                        for q in range(TCSB):
                            v.bn_stats(stats[:, q, ob, :], psum_a[:, q, :])
                            if nevac >= 4:
                                v.wait_ge(S["ffst"], 16 * (nevac - 3))
                            v.tensor_copy(stg[:, nevac % 4], psum_a[:, q, :])
                            v.drain().then_inc(S["stg"], 1)
                            nevac += 1
                    v.drain()
                    for q in range(TCSB):
                        tc = sb * TCSB + q
                        v.bn_aggr(m2[:, q], stats[:, q].rearrange("p a b -> p (a b)"))
                        v.drain()
                        v.tensor_scalar_add(vrec2[:, q:q + 1], m2[:, q, 1:2], EPS)
                        v.drain().then_inc(S["ln2b"], 1)
                        v.wait_ge(S["nrm"], tc + 1)
                        v.tensor_scalar(nmrs[:, tc:tc + 1], m2[:, q, 0:1],
                                        rs2[:, tc:tc + 1], -1.0,
                                        ALU.mult, ALU.mult)
                        v.drain().then_inc(S["nmrs"], 1)
                    for q in range(TCSB):
                        tc = sb * TCSB + q
                        for f in range(8):
                            idx = tc * 8 + f
                            v.wait_ge(S["rl2"],
                                      16 * min(idx + 3, 64 * (sb + 1)))
                            if idx >= 4:
                                v.wait_ge(S["outd"], 16 * (idx - 3))
                            v.tensor_scalar(outb[:, idx % 4], rlb[:, idx % 4],
                                            rs2[:, tc:tc + 1],
                                            nmrs[:, tc:tc + 1],
                                            ALU.mult, ALU.add)
                            v.drain().then_inc(S["nrmd"], 1)

            @blk.scalar
            def _(s):
                for sb in range(NSB):
                    s.wait_ge(S["exp1"], sb + 1)
                    s.activation(prb[:], prb[:], AF.Exp)
                    s.drain().then_inc(S["exp2"], 1)
                    for q in range(TCSB):
                        tc = sb * TCSB + q
                        s.wait_ge(S["ln2b"], tc + 1)
                        s.activation(lt2[:], vrec2[:, q:q + 1], AF.Ln)
                        s.drain()
                        s.activation(rs2[:, tc:tc + 1], lt2[:],
                                     AF.Exp, scale=-0.5)
                        s.drain().then_inc(S["nrm"], 1)


            @blk.sync
            def _(sy):
                nst = 0
                nidx = 0
                for sb in range(NSB):
                    for ob in range(NOB):
                        for q in range(TCSB):
                            tc = sb * TCSB + q
                            sy.wait_ge(S["stg"], nst + 1)
                            sy.dma_start(
                                ffnout.ap()[tc * 128:(tc + 1) * 128,
                                            ob * 512:(ob + 1) * 512],
                                stg[:, nst % 4],
                            ).then_inc(S["ffst"], 16)
                            nst += 1
                    sy.wait_ge(S["ffst"], 16 * 64 * (sb + 1))
                    for q in range(TCSB):
                        tc = sb * TCSB + q
                        for f in range(8):
                            idx = tc * 8 + f
                            if idx >= 4:
                                sy.wait_ge(S["nrmd"], idx - 3)
                            sy.dma_start(rlb[:, idx % 4],
                                         ffnout.ap()[tc * 128:(tc + 1) * 128,
                                                     f * 512:(f + 1) * 512]
                                         ).then_inc(S["rl2"], 16)
                            if idx >= 4:
                                pidx = idx - 4
                                sy.wait_ge(S["nrmd"], pidx + 2)
                                ptc, pf = pidx // 8, pidx % 8
                                sy.dma_start(
                                    out.ap()[ptc * 128:(ptc + 1) * 128,
                                             pf * 512:(pf + 1) * 512],
                                    outb[:, pidx % 4],
                                ).then_inc(S["outd"], 16)
                for pidx in range(NTC * 8 - 4, NTC * 8):
                    sy.wait_ge(S["nrmd"], min(pidx + 2, NTC * 8))
                    ptc, pf = pidx // 8, pidx % 8
                    sy.dma_start(out.ap()[ptc * 128:(ptc + 1) * 128,
                                          pf * 512:(pf + 1) * 512],
                                 outb[:, pidx % 4]).then_inc(S["outd"], 16)
                sy.wait_ge(S["outd"], 16 * NTC * 8)

    stp.close()
    return nc


def kernel(L0_attn_x, L0_ffn_W_base, L0_ffn_b_base, L0_ffn_W_router,
           L0_ffn_lora_A, L0_ffn_lora_B):
    NB = ml_dtypes.bfloat16
    xf = np.ascontiguousarray(np.asarray(L0_attn_x, dtype=np.float32).reshape(16384, D))
    wbT_h = np.ascontiguousarray(np.asarray(L0_ffn_W_base, np.float32).T)
    bb_h = np.ascontiguousarray(np.asarray(L0_ffn_b_base, np.float32))
    wrT_h = np.ascontiguousarray(np.asarray(L0_ffn_W_router, np.float32).T)
    acatT_h = np.ascontiguousarray(
        np.asarray(L0_ffn_lora_A, np.float32).reshape(128, D).T)
    bcat_h = np.ascontiguousarray(
        np.asarray(L0_ffn_lora_B, np.float32).transpose(0, 2, 1).reshape(128, D))
    identb_h = np.eye(128, dtype=NB)
    identf_h = np.eye(128, dtype=np.float32)
    eopat_h = (np.arange(128)[None, :] // 16 ==
               np.arange(8)[:, None]).astype(NB)
    ones1_h = np.ones((1, 128), dtype=NB)

    nc = build_nc()
    in_maps = []
    for c in range(8):
        in_maps.append(dict(
            xs=np.ascontiguousarray(xf[c * R:(c + 1) * R]),
            wbT=wbT_h, bb=bb_h, wrT=wrT_h, acatT=acatT_h, bcat=bcat_h,
            identb=identb_h, identf=identf_h, eopat=eopat_h, ones1=ones1_h,
        ))
    res = run_bass_kernel_spmd(nc, in_maps, core_ids=list(range(8)))
    outs = [res.results[c]["out"] for c in range(8)]
    return np.concatenate(outs, axis=0).reshape(4, 4096, 4096).astype(np.float32)
